# version 04
# baseline (speedup 1.0000x reference)
from contextlib import ExitStack

import numpy as np
import concourse.bass as bass
import concourse.mybir as mybir
from concourse.bass_utils import run_bass_kernel_spmd

F32 = mybir.dt.float32
BF16 = mybir.dt.bfloat16
I8 = mybir.dt.int8
AF = mybir.ActivationFunctionType
ALU = mybir.AluOpType
AX = mybir.AxisListType

B, S, C, H, DH, I = 4, 4096, 2048, 16, 128, 8192
N, M = 2048, 4096
NT, MT, CJ, IJ = 16, 32, 16, 64
EPS = 1e-5
SCALE = 1.0 / float(np.float32(np.sqrt(128.0)))
EXP_BIAS = -8.0
NSTRIPE = 8


SERIAL = [False]
SERIAL2 = [True]
FRONTIER = [False]
_PLANBOX = [None]


class Plan:
    ENGS = ("T", "V", "S", "G", "Y")

    def __init__(self):
        self.ops = {e: [] for e in self.ENGS}
        self.cnt = {}
        self.gidx = 0
        self.last = None

    def _push(self, eng, fn, waits, sem, inc):
        self.cnt[sem] = self.cnt.get(sem, 0) + inc
        if SERIAL[0]:
            waits = list(waits) + ([self.last] if self.last else [])
        wmap = {}
        for w in waits:
            if w is None:
                continue
            s, v = w[0], w[1]
            if v > wmap.get(s, -1):
                wmap[s] = v
        gafter = self.gidx + inc
        fr = 0
        for w in waits:
            if w is None:
                continue
            if len(w) >= 4 and w[2] == eng:
                continue
            if len(w) >= 4 and w[3] > fr:
                fr = w[3]
        self.ops[eng].append((sorted(wmap.items()), fn, sem, inc, self.gidx, fr))
        self.gidx = gafter
        self.last = (sem, self.cnt[sem], eng, gafter)
        return self.last

    def op(self, eng, fn, waits=()):
        return self._push(eng, fn, waits, eng, 1)

    def dma(self, eng, fn, waits=()):
        return self._push(eng, fn, waits, {"G": "DG", "Y": "DY"}[eng], 16)


DEBUG = False


def _build():
    nc = bass.Bass("TRN2", target_bir_lowering=False, debug=False)
    d = {}

    def din(name, shape):
        d[name] = nc.dram_tensor(name, list(shape), F32, kind="ExternalInput").ap()
        return d[name]

    x_full = din("x_full", [M, C])
    din("wq_r", [H, 128, C])
    din("wk_r", [128, C])
    din("wv_r", [128, C])
    din("wo_r", [CJ, 128, C])
    w_gate = din("w_gate", [I, C])
    w_up = din("w_up", [I, C])
    w_down = din("w_down", [C, I])
    din("bq_c", [128, H])
    din("bk_c", [128, 1])
    din("bv_c", [128, 1])
    din("bo_r", [1, C])
    din("ones_c", [128, 1])
    din("ones_r", [1, 128])
    din("ident_h", [128, 128])

    out_d = nc.dram_tensor("out", [N, C], F32, kind="ExternalOutput").ap()
    dbg = {}
    if DEBUG:
        for nm, shp, dt in [("dbg_KT", [128, M], F32), ("dbg_Vb", [128, M], F32),]:
            pass
        for nm, shp, dt in [("dbg_KT", [128, M], F32), ("dbg_Vb", [128, M], F32),
                            ("dbg_outT", [128, 32768], F32), ("dbg_x2", [N, C], F32),
                            ("dbg_stats", [128, 64], F32),
                            ("dbg_wqg", [16, CJ, 128, 512], I8),
                            ("dbg_wqd", [8, IJ, 128, 256], I8),
                            ("dbg_xqT", [128, CJ * 256], F32),
                            ("dbg_xnT", [128, 32768], F32),
                            ("dbg_f32p", [128, 1024], F32),
                            ("dbg_early", [128, 16], F32),
                            ("dbg_early2", [128, 16], F32),
                            ("dbg_early3", [128, 64], F32),
                            ("dbg_snap", [12, 128, 16], "BF16"),
                            ("dbg_QTh", [128, N], F32)]:
            dbg[nm] = nc.dram_tensor(nm, shp, BF16 if dt == "BF16" else dt,
                                     kind="ExternalOutput").ap()
    x2_d = nc.dram_tensor("x2_scr", [N, C], F32, kind="Internal").ap()
    wqg_d = nc.dram_tensor("wqg_scr", [16, CJ, 128, 512], I8, kind="Internal").ap()
    wqu_d = nc.dram_tensor("wqu_scr", [16, CJ, 128, 512], I8, kind="Internal").ap()
    wqd_d = nc.dram_tensor("wqd_scr", [8, IJ, 128, 256], I8, kind="Internal").ap()

    P = Plan()
    st = ExitStack()
    sb = lambda name, shape, dt: st.enter_context(nc.sbuf_tensor(name, shape, dt))
    psa = lambda name, shape, dt: st.enter_context(nc.psum_tensor(name, shape, dt))

    bq_sb = sb("bq_sb", [128, H], F32)
    bk_sb = sb("bk_sb", [128, 1], F32)
    bv_bf = sb("bv_bf", [128, 1], BF16)
    ones_sb = sb("ones_sb", [128, 1], F32)
    onesr_sb = sb("onesr_sb", [1, 128], F32)
    ident_bf = sb("ident_bf", [128, 128], BF16)
    eps_sb = sb("eps_sb", [128, 1], F32)
    nbias_sb = sb("nbias_sb", [128, 1], F32)
    mean_t = [sb(f"mean_t{i}", [128, 1], F32) for i in range(2)]
    ssqa_t = [sb(f"ssqa_t{i}", [128, 1], F32) for i in range(2)]
    ssqb_t = [sb(f"ssqb_t{i}", [128, 1], F32) for i in range(2)]
    var_t = [sb(f"var_t{i}", [128, 1], F32) for i in range(2)]
    rstd_t = [sb(f"rstd_t{i}", [128, 1], F32) for i in range(2)]
    negmr_t = [sb(f"negmr_t{i}", [128, 1], F32) for i in range(2)]
    s1_2 = sb("s1_2", [128, NT], F32)
    ssq_2 = sb("ssq_2", [128, NT], F32)
    mean2 = sb("mean2", [128, NT], F32)
    var2 = sb("var2", [128, NT], F32)
    rstd2 = sb("rstd2", [128, NT], F32)
    negmr2 = sb("negmr2", [128, NT], F32)
    wsum_p = sb("wsum_p", [128, 64], F32)
    wsum_1 = sb("wsum_1", [128, 1], F32)
    rws_bc = [sb(f"rws_bc{i}", [128, 1], F32) for i in range(3)]
    ws_bc = [sb(f"ws_bc{i}", [128, 1], F32) for i in range(3)]
    ws_tmp = sb("ws_tmp", [1, 1], F32)
    rws_tmp = sb("rws_tmp", [1, 1], F32)
    mx_q = sb("mx_q", [128, 1], F32)
    sx_q = [sb(f"sx_q{i}", [128, 1], F32) for i in range(2)]
    rsx_q = [sb(f"rsx_q{i}", [128, 1], F32) for i in range(2)]
    sg_q = [sb(f"sg_q{i}", [128, 1], F32) for i in range(2)]
    su_q = [sb(f"su_q{i}", [128, 1], F32) for i in range(2)]
    sh_q = [sb(f"sh_q{i}", [128, 1], F32) for i in range(2)]
    rsh_q = [sb(f"rsh_q{i}", [128, 1], F32) for i in range(2)]
    hmx = sb("hmx", [128, 32], F32)
    hmx1 = [sb(f"hmx1{i}", [128, 1], F32) for i in range(2)]

    arenaA = sb("arenaA", [128, 32768], BF16)
    arenaB = sb("arenaB", [128, 32768], BF16)
    MEGA = sb("MEGA", [128, 78848], I8)

    def mg(off, nbytes, dt, rows=128):
        return MEGA[0:rows, off:off + nbytes].bitcast(dt)

    KT = mg(0, 8192, BF16)
    Vb = mg(8192, 8192, BF16)
    xin = [mg(16384, 8192, F32), mg(24576, 8192, F32)]
    xn_bf = [mg(32768, 4096, BF16), mg(36864, 4096, BF16)]
    QTh = mg(40960, 4096, BF16)
    xnT_tmp = QTh
    ET = [mg(45056 + i * 2048, 2048, BF16) for i in range(3)]
    Sacc_v = mg(51200, 4096, F32)
    Sacc_g = mg(55296, 4096, F32)
    rs_sb = mg(59392, 4096, F32, rows=1)
    rs_bc = mg(63488, 4096, F32)
    Wqh = [mg(67584, 4096, BF16), mg(71680, 4096, BF16)]
    Wk_bf = Wqh[0]
    Wv_bf = Wqh[1]
    sqdK = rs_bc

    arenaC = mg(0, 65536, F32)
    xq_bf = mg(61440, 4096, BF16)
    wtile_bf = mg(65536, 4096, BF16)
    xqT = mg(65536, 8192, BF16)
    up_t = [mg(73728, 2048, F32), mg(75776, 2048, F32)]
    i8_tmp = mg(77824, 1024, I8)
    bo_bc = arenaC[:, 8192:10240]
    sqd2 = arenaC[:, 10240:12288]
    bo_sb = arenaC[0:1, 12288:14336]
    bo_eff = arenaC[0:1, 14336:16384]
    wpass = [arenaC[:, 0:2048], arenaC[:, 2048:4096]]

    ps_sc = [psa(f"ps_sc{i}", [128, 1024], F32) for i in range(2)]
    ps_pv = psa("ps_pv", [128, 1024], F32)
    ps_aux = psa("ps_aux", [128, 512], F32)
    ps_trA = psa("ps_trA", [128, 1024], BF16)

    xnT = arenaA
    xnT3 = xnT.rearrange("p (j q) -> p j q", j=CJ)
    outT = arenaB
    outT3 = outT.rearrange("p (j q) -> p j q", j=CJ)
    ps_tr3 = ps_trA.rearrange("p (j q) -> p j q", j=8)

    P.dma("Y", lambda e: e.dma_start(out=bq_sb[:], in_=d["bq_c"][:, :]))
    P.dma("Y", lambda e: e.dma_start(out=bk_sb[:], in_=d["bk_c"][:, :]))
    m_ones = P.dma("Y", lambda e: e.dma_start(out=ones_sb[:], in_=d["ones_c"][:, :]))
    m_onesr = P.dma("Y", lambda e: e.dma_start(out=onesr_sb[:], in_=d["ones_r"][:, :]))
    m_ident = P.dma("G", lambda e: e.dma_start(out=ident_bf[:], in_=d["ident_h"][:, :]))
    m_bv = P.dma("G", lambda e: e.dma_start(out=bv_bf[:], in_=d["bv_c"][:, :]))
    m_wkb = P.dma("G", lambda e: e.dma_start(out=Wk_bf[:], in_=d["wk_r"][:, :]))
    m_wvb = P.dma("G", lambda e: e.dma_start(out=Wv_bf[:], in_=d["wv_r"][:, :]))
    mset1 = P.op("V", lambda e: e.memset(eps_sb[:], EPS))
    mset2 = P.op("V", lambda e: e.memset(nbias_sb[:], EXP_BIAS))

    m_xdma = [None] * MT
    m_xn = [None] * MT
    m_tr_ev = {}
    m_kv_done = [None] * MT
    last_V_evac = None
    last_K_evac = None

    for mt in range(MT):
        bi = mt % 2
        SNAP = [0]
        def snap(marker):
            if not (DEBUG and mt == 2):
                return marker
            k = SNAP[0]; SNAP[0] += 1
            return P.dma("Y", lambda e, k=k: e.dma_start(
                out=dbg["dbg_snap"][k, :, :], in_=KT[:, 0:16]), waits=[marker])
        m_xdma[mt] = P.dma("Y", lambda e, mt=mt, bi=bi: e.dma_start(
            out=xin[bi], in_=x_full[mt * 128:(mt + 1) * 128, :]),
            waits=[m_xn[mt - 2]] if mt >= 2 else [])
        m_xdma[mt] = snap(m_xdma[mt]) if (DEBUG and mt == 2) else m_xdma[mt]
        m_s1 = P.op("V", lambda e, bi=bi: e.reduce_sum(mean_t[bi][:], xin[bi], AX.X),
                    waits=[m_xdma[mt]])
        m_sqa = P.op("S", lambda e, bi=bi: e.activation(
            sqdK[:], xin[bi][:, 0:1024], AF.Square, accum_out=ssqa_t[bi][:]),
            waits=[m_xdma[mt]])
        m_sqb = P.op("S", lambda e, bi=bi: e.activation(
            sqdK[:], xin[bi][:, 1024:2048], AF.Square, accum_out=ssqb_t[bi][:]),
            waits=[m_xdma[mt]])

        m_v1 = P.op("V", lambda e, bi=bi: e.tensor_scalar(
            mean_t[bi][:], mean_t[bi][:], 1.0 / 2048.0, None, ALU.mult), waits=[m_s1])
        m_v2 = P.op("V", lambda e, bi=bi: e.scalar_tensor_tensor(
            var_t[bi][:], mean_t[bi][:], -1.0, mean_t[bi][:], ALU.mult, ALU.mult),
            waits=[m_v1])
        m_v3a = P.op("V", lambda e, bi=bi: e.scalar_tensor_tensor(
            var_t[bi][:], ssqa_t[bi][:], 1.0 / 2048.0, var_t[bi][:], ALU.mult, ALU.add),
            waits=[m_v2, m_sqa])
        m_v3 = P.op("V", lambda e, bi=bi: e.scalar_tensor_tensor(
            var_t[bi][:], ssqb_t[bi][:], 1.0 / 2048.0, var_t[bi][:], ALU.mult, ALU.add),
            waits=[m_v3a, m_sqb])
        m_sd = P.op("S", lambda e, bi=bi: e.activation(
            var_t[bi][:], var_t[bi][:], AF.Sqrt, bias=eps_sb[:]), waits=[m_v3, mset1])
        if DEBUG and False:
            mde = P.op("V", lambda e: e.tensor_copy(wsum_p[:, 0:1], eps_sb[:]), waits=[m_sd])
            mde = P.op("V", lambda e: e.tensor_copy(wsum_p[:, 1:2], nbias_sb[:]), waits=[mde])
            mde = P.op("V", lambda e: e.tensor_copy(wsum_p[:, 2:3], mean_t[0][:]), waits=[mde])
            mde = P.op("V", lambda e: e.tensor_copy(wsum_p[:, 3:4], ssqa_t[0][:]), waits=[mde])
            mde = P.op("V", lambda e: e.tensor_copy(wsum_p[:, 4:5], ssqb_t[0][:]), waits=[mde])
            mde = P.op("V", lambda e: e.tensor_copy(wsum_p[:, 5:6], var_t[0][:]), waits=[mde])
            mde = P.op("V", lambda e: e.tensor_copy(wsum_p[:, 6:7], ones_sb[:]), waits=[mde])
            mde = P.op("V", lambda e: e.tensor_copy(wsum_p[:, 7:8], bk_sb[:]), waits=[mde])
            mde_d = P.dma("Y", lambda e: e.dma_start(out=dbg["dbg_early"][:, :],
                                                     in_=wsum_p[:, 0:16]), waits=[mde])
            m_sd = mde_d
        m_rs = P.op("V", lambda e, bi=bi: e.reciprocal(rstd_t[bi][:], var_t[bi][:]),
                    waits=[m_sd])
        m_nm = P.op("V", lambda e, bi=bi: e.scalar_tensor_tensor(
            negmr_t[bi][:], mean_t[bi][:], -1.0, rstd_t[bi][:], ALU.mult, ALU.mult),
            waits=[m_rs])
        xw = [m_nm]
        if mt >= 2:
            xw.append(m_tr_ev[(mt - 2, 1)])
        m_xn[mt] = P.op("S", lambda e, bi=bi: e.activation(
            xn_bf[bi][:], xin[bi], AF.Identity, bias=negmr_t[bi][:], scale=rstd_t[bi][:]),
            waits=xw)
        m_xn[mt] = snap(m_xn[mt]) if (DEBUG and mt == 2) else m_xn[mt]

        for g in range(2):
            trw = [m_xn[mt], m_ident]
            if (mt, g) != (0, 0):
                trw.append(m_tr_ev[(mt, g - 1) if g == 1 else (mt - 1, 1)])
            m_tr = None
            for j8 in range(8):
                j = g * 8 + j8
                m_tr = P.op("T", lambda e, bi=bi, j=j, j8=j8: e.transpose(
                    ps_trA[:, j8 * 128:(j8 + 1) * 128],
                    xn_bf[bi][:, j * 128:(j + 1) * 128], ident_bf[:]),
                    waits=trw if j8 == 0 else ())
            if mt < NT:
                mlast = None
                for j8 in range(8):
                    j = g * 8 + j8
                    mlast = P.op("V", lambda e, j=j, j8=j8, mt=mt: e.tensor_copy(
                        xnT3[:, j, mt * 128:(mt + 1) * 128],
                        ps_trA[:, j8 * 128:(j8 + 1) * 128]), waits=[m_tr] if j8 == 0 else ())
                m_tr_ev[(mt, g)] = snap(mlast) if (DEBUG and mt == 2) else mlast
            else:
                m_tr_ev[(mt, g)] = P.op("V", lambda e, g=g: e.tensor_copy(
                    xnT_tmp[:, g * 1024:(g + 1) * 1024], ps_trA[:, :]), waits=[m_tr])

        if mt < NT:
            rhs_at = lambda j, mt=mt: xnT[:, j * 2048 + mt * 128: j * 2048 + (mt + 1) * 128]
        else:
            rhs_at = lambda j: xnT_tmp[:, j * 128:(j + 1) * 128]
        kw = [m_tr_ev[(mt, 0)], m_tr_ev[(mt, 1)], m_wkb, m_wvb]
        if mt >= 1:
            kw.append(m_kv_done[mt - 1])
        m_k = None
        for j in range(CJ):
            m_k = P.op("T", lambda e, j=j, rhs_at=rhs_at: e.matmul(
                ps_aux[:, 0:128], Wk_bf[:, j * 128:(j + 1) * 128], rhs_at(j),
                start=(j == 0), stop=(j == CJ - 1)), waits=kw if j == 0 else ())
        m_k = snap(m_k) if (DEBUG and mt == 2) else m_k
        m_kev = P.op("S", lambda e, mt=mt: e.activation(
            KT[:, mt * 128:(mt + 1) * 128], ps_aux[:, 0:128], AF.Identity, bias=bk_sb[:]),
            waits=[m_k])
        m_kev = snap(m_kev) if (DEBUG and mt == 2) else m_kev
        if DEBUG and mt == 0:
            mke = P.op("V", lambda e: e.tensor_copy(wsum_p[:, 0:64], KT[:, 0:64]),
                       waits=[m_kev])
            mke = P.dma("Y", lambda e: e.dma_start(out=dbg["dbg_early3"][:, :],
                                                   in_=wsum_p[:, 0:64]), waits=[mke])
            m_kev = mke
        m_v = None
        for j in range(CJ):
            m_v = P.op("T", lambda e, j=j, rhs_at=rhs_at: e.matmul(
                ps_aux[:, 0:128], rhs_at(j), Wv_bf[:, j * 128:(j + 1) * 128],
                start=(j == 0), stop=(j == CJ - 1)),
                waits=[m_kev] if j == 0 else ())
        m_vev = P.op("V", lambda e, mt=mt: e.tensor_copy(
            Vb[:, mt * 128:(mt + 1) * 128], ps_aux[:, 0:128]), waits=[m_v])
        if DEBUG and mt in (1,):
            nm_ = "dbg_early" if mt == 1 else "dbg_early2"
            mvv = P.op("V", lambda e: e.tensor_copy(wsum_p[:, 48:64], KT[:, 0:16]),
                       waits=[m_vev])
            mvv = P.dma("Y", lambda e, nm_=nm_: e.dma_start(out=dbg[nm_][:, :],
                                                            in_=wsum_p[:, 48:64]),
                        waits=[mvv])
            m_vev = mvv
            m_kv_done[mt] = mvv
        m_kv_done[mt] = m_vev
        last_V_evac = m_vev
        last_K_evac = m_kev

    phaseK_gate = []
    if DEBUG:
        mv1 = P.op("V", lambda e: e.tensor_copy(sqdK[:, 0:128], ident_bf[:]),
                   waits=[m_kv_done[MT - 1], last_K_evac])
        mv2 = P.op("V", lambda e: e.tensor_copy(sqdK[:, 128:640], KT[:, 0:512]),
                   waits=[mv1])
        mv3 = P.op("V", lambda e: e.tensor_copy(sqdK[:, 640:896], xnT3[:, 0, 0:256]),
                   waits=[mv2])
        mv4 = P.op("V", lambda e: e.tensor_copy(sqdK[:, 896:960], Wk_bf[:, 0:64]),
                   waits=[mv3])
        mv4 = P.op("V", lambda e: e.tensor_copy(sqdK[:, 960:1024], Vb[:, 0:64]),
                   waits=[mv4])
        mfp = P.dma("Y", lambda e: e.dma_start(out=dbg["dbg_f32p"][:, :], in_=sqdK[:]),
                    waits=[mv4])
        phaseK_gate.append(mfp)
        P.dma("G", lambda e: e.dma_start(out=dbg["dbg_KT"][:, :], in_=KT[:]),
              waits=[m_kv_done[MT - 1], last_K_evac])
        P.dma("G", lambda e: e.dma_start(out=dbg["dbg_Vb"][:, :], in_=Vb[:]),
              waits=[m_kv_done[MT - 1]])
        mk3 = P.dma("G", lambda e: e.dma_start(out=dbg["dbg_xnT"][:, :], in_=xnT[:, :]),
                    waits=[m_kv_done[MT - 1]])
        phaseK_gate = [mk3]
    m_wq = [None, None]
    m_wq[0] = P.dma("G", lambda e: e.dma_start(out=Wqh[0][:], in_=d["wq_r"][0, :, :]),
                    waits=[m_kv_done[MT - 1]] + phaseK_gate)
    m_qt_done = {}
    m_exp = {}
    m_pv = {}
    m_norm_ev = {}
    m_last_qt_mm = None
    GSET = set(range(2, 32, 3))

    for h in range(H):
        hb = h % 2
        if h + 1 < H:
            m_wq[(h + 1) % 2] = P.dma("G", lambda e, h=h: e.dma_start(
                out=Wqh[(h + 1) % 2][:], in_=d["wq_r"][h + 1, :, :]),
                waits=[m_qt_done[(h - 1, 3)]] if h >= 1 else [m_kv_done[MT - 1]])
        for qs4 in range(4):
            w0 = [m_wq[hb], m_tr_ev[(NT - 1, 1)], last_K_evac]
            if qs4 >= 1:
                w0.append(m_qt_done[(h, qs4 - 1)])
            elif h >= 1:
                w0.append(m_exp[h * 2 * MT - 1])
            mm = None
            for j in range(CJ):
                mm = P.op("T", lambda e, j=j, qs4=qs4, hb=hb: e.matmul(
                    ps_sc[0][:, 0:512],
                    Wqh[hb][:, j * 128:(j + 1) * 128],
                    xnT[:, j * 2048 + qs4 * 512: j * 2048 + (qs4 + 1) * 512],
                    start=(j == 0), stop=(j == CJ - 1)), waits=w0 if j == 0 else ())
            m_last_qt_mm = mm
            m_qt_done[(h, qs4)] = P.op("S", lambda e, qs4=qs4, h=h: e.activation(
                QTh[:, qs4 * 512:(qs4 + 1) * 512], ps_sc[0][:, 0:512], AF.Identity,
                bias=bq_sb[:, h:h + 1]), waits=[mm])

        for qw in range(2):
            qoff = qw * 1024
            prev = (h, qw - 1) if qw == 1 else (h - 1, 1)
            have_prev = (h, qw) != (0, 0)
            m_z_v = P.op("V", lambda e: e.memset(Sacc_v[:], 0.0),
                         waits=[m_norm_ev[prev]] if have_prev else [])
            m_z_g = P.op("G", lambda e: e.memset(Sacc_g[:], 0.0),
                         waits=[m_norm_ev[prev]] if have_prev else [])
            last_v_sadd = None
            last_g_sadd = None
            for c in range(MT):
                gidx = (h * 2 + qw) * MT + c
                pb = ps_sc[c % 2]
                scw = []
                if c == 0:
                    scw = [m_qt_done[(h, 3)], last_K_evac]
                if gidx >= 2:
                    scw.append(m_exp[gidx - 2])
                mm = None
                for half in range(2):
                    mm = P.op("T", lambda e, c=c, pb=pb, qoff=qoff, half=half: e.matmul(
                        pb[:, half * 512:(half + 1) * 512],
                        KT[:, c * 128:(c + 1) * 128],
                        QTh[:, qoff + half * 512: qoff + (half + 1) * 512],
                        start=True, stop=True), waits=scw if half == 0 else ())
                ew = [mm, mset2]
                if gidx >= 3:
                    ew += [m_pv[gidx - 3]]
                    ew += [m_sadd4] if (m_sadd4 := m_exp.get(("sadd", gidx - 3))) else []
                m_exp[gidx] = P.op("S", lambda e, c=c, pb=pb: e.activation(
                    ET[c % 3][:], pb[:, :], AF.Exp, bias=nbias_sb[:], scale=SCALE),
                    waits=ew)
                if c in GSET:
                    msadd = P.op("G", lambda e, c=c: e.tensor_tensor(
                        Sacc_g[:], Sacc_g[:], ET[c % 3][:], ALU.add),
                        waits=[m_exp[gidx], m_z_g])
                    last_g_sadd = msadd
                else:
                    msadd = P.op("V", lambda e, c=c: e.tensor_tensor(
                        Sacc_v[:], Sacc_v[:], ET[c % 3][:], ALU.add),
                        waits=[m_exp[gidx], m_z_v])
                    last_v_sadd = msadd
                m_exp[("sadd", gidx)] = msadd
                pvw = [m_exp[gidx]]
                if c == 0:
                    pvw.append(last_V_evac)
                    if have_prev:
                        pvw.append(m_norm_ev[prev])
                mmv = None
                for half in range(2):
                    mmv = P.op("T", lambda e, c=c, half=half: e.matmul(
                        ps_pv[:, half * 512:(half + 1) * 512],
                        Vb[:, c * 128:(c + 1) * 128],
                        ET[c % 3][:, half * 512:(half + 1) * 512],
                        start=(c == 0), stop=(c == MT - 1)),
                        waits=pvw if half == 0 else ())
                m_pv[gidx] = mmv
            m_rec = [None, None]
            for half in range(2):
                sw = [last_v_sadd, last_g_sadd, m_ones]
                if half == 1:
                    sw.append(m_rec[0])
                mm1 = P.op("T", lambda e, half=half: e.matmul(
                    ps_aux[0:1, 0:512], ones_sb[:], Sacc_v[:, half * 512:(half + 1) * 512],
                    start=True, stop=False), waits=sw)
                mm2 = P.op("T", lambda e, half=half: e.matmul(
                    ps_aux[0:1, 0:512], ones_sb[:], Sacc_g[:, half * 512:(half + 1) * 512],
                    start=False, stop=True))
                m_rec[half] = P.op("V", lambda e, half=half: e.reciprocal(
                    rs_sb[0:1, half * 512:(half + 1) * 512], ps_aux[0:1, 0:512]),
                    waits=[mm2])
            m_bc_ev = [None, None]
            for half in range(2):
                bw = [m_rec[0], m_rec[1], m_onesr]
                if half == 1:
                    bw.append(m_bc_ev[0])
                mmb = P.op("T", lambda e, half=half: e.matmul(
                    ps_aux[:, 0:512], onesr_sb[:], rs_sb[0:1, half * 512:(half + 1) * 512],
                    start=True, stop=True), waits=bw)
                m_bc_ev[half] = P.op("V", lambda e, half=half: e.tensor_copy(
                    rs_bc[:, half * 512:(half + 1) * 512], ps_aux[:, 0:512]), waits=[mmb])
            last_g = (h * 2 + qw) * MT + MT - 1
            for half in range(2):
                m_norm_ev[(h, qw, half)] = P.op("V", lambda e, h=h, qoff=qoff, half=half: e.tensor_tensor(
                    outT[:, h * 2048 + qoff + half * 512: h * 2048 + qoff + (half + 1) * 512],
                    ps_pv[:, half * 512:(half + 1) * 512],
                    rs_bc[:, half * 512:(half + 1) * 512], ALU.mult),
                    waits=[m_pv[last_g], m_bc_ev[half]])
            m_norm_ev[(h, qw)] = m_norm_ev[(h, qw, 1)]
    m_attn_done = m_norm_ev[(H - 1, 1)]
    m_last_bc = m_bc_ev[1]

    dump_gate = []
    if DEBUG:
        P.dma("G", lambda e: e.dma_start(out=dbg["dbg_outT"][:, :], in_=outT[:, :]),
              waits=[m_attn_done])
        mdq = P.dma("G", lambda e: e.dma_start(out=dbg["dbg_QTh"][:, :], in_=QTh[:]),
                    waits=[m_attn_done])
        dump_gate = [mdq]

    Wo_bf = arenaA
    Wo3 = Wo_bf.rearrange("p (j f) -> p j f", j=CJ)
    m_wo = [None] * CJ
    for j in range(CJ):
        m_wo[j] = P.dma("G", lambda e, j=j: e.dma_start(
            out=Wo_bf[:, j * 2048:(j + 1) * 2048], in_=d["wo_r"][j, :, :]),
            waits=[m_last_qt_mm])

    m_bo_ld = P.dma("Y", lambda e: e.dma_start(out=bo_sb, in_=d["bo_r"][:, :]),
                    waits=[m_attn_done])
    m_bvwo_ev = None
    for w4 in range(4):
        mmc = None
        for j in range(CJ):
            ww = []
            if j == 0:
                ww = [m_wo[CJ - 1], m_bv, m_last_bc]
                if w4 > 0:
                    ww.append(m_bvwo_ev)
            mmc = P.op("T", lambda e, j=j, w4=w4: e.matmul(
                ps_aux[0:1, 0:512], bv_bf[:],
                Wo_bf[:, j * 2048 + w4 * 512: j * 2048 + (w4 + 1) * 512],
                start=(j == 0), stop=(j == CJ - 1)), waits=ww)
        m_bvwo_ev = P.op("V", lambda e, w4=w4: e.tensor_copy(
            bo_eff[:, w4 * 512:(w4 + 1) * 512], ps_aux[0:1, 0:512]), waits=[mmc])
    m_boeff = P.op("V", lambda e: e.tensor_tensor(bo_eff, bo_eff, bo_sb, ALU.add),
                   waits=[m_bvwo_ev, m_bo_ld])
    m_bobc_ev = None
    for w4 in range(4):
        mmb = P.op("T", lambda e, w4=w4: e.matmul(
            ps_aux[:, 0:512], onesr_sb[:], bo_eff[:, w4 * 512:(w4 + 1) * 512],
            start=True, stop=True),
            waits=[m_boeff] + ([m_bobc_ev] if w4 else []))
        m_bobc_ev = P.op("V", lambda e, w4=w4: e.tensor_copy(
            bo_bc[:, w4 * 512:(w4 + 1) * 512], ps_aux[:, 0:512]), waits=[mmb])

    xres = [arenaC[:, 0:2048], arenaC[:, 2048:4096]]
    x2b = [arenaC[:, 4096:6144], arenaC[:, 6144:8192]]
    m_x2_dma = [None] * NT
    m_y_ev = {}
    m_wo_last_mm = None
    for qt in range(NT):
        bi = qt % 2
        xw = [m_attn_done] + dump_gate
        if qt >= 2:
            xw.append(m_y_ev[(qt - 2, 3)])
        m_xr = P.dma("Y", lambda e, qt=qt, bi=bi: e.dma_start(
            out=xres[bi], in_=x_full[qt * 128:(qt + 1) * 128, :]), waits=xw)
        for w4 in range(4):
            w0 = []
            if qt == 0 and w4 == 0:
                w0 = [m_wo[j2] for j2 in range(CJ)] + [m_attn_done, m_bobc_ev]
            elif w4 >= 2:
                w0 = [m_y_ev[(qt, w4 - 2)]]
            elif qt >= 1:
                w0 = [m_y_ev[(qt - 1, w4 + 2)]]
            mm = None
            for j in range(CJ):
                mm = P.op("T", lambda e, j=j, w4=w4, qt=qt: e.matmul(
                    ps_pv[:, (w4 % 2) * 512:(w4 % 2 + 1) * 512],
                    outT[:, j * 2048 + qt * 128: j * 2048 + (qt + 1) * 128],
                    Wo_bf[:, j * 2048 + w4 * 512: j * 2048 + (w4 + 1) * 512],
                    start=(j == 0), stop=(j == CJ - 1)), waits=w0 if j == 0 else ())
            m_wo_last_mm = mm
            m_t1 = P.op("V", lambda e, bi=bi, w4=w4: e.tensor_tensor(
                x2b[bi][:, w4 * 512:(w4 + 1) * 512],
                ps_pv[:, (w4 % 2) * 512:(w4 % 2 + 1) * 512],
                bo_bc[:, w4 * 512:(w4 + 1) * 512], ALU.add),
                waits=[mm, m_xr] + ([m_x2_dma[qt - 2]] if qt >= 2 else []))
            m_y_ev[(qt, w4)] = P.op("V", lambda e, bi=bi, w4=w4: e.tensor_tensor(
                x2b[bi][:, w4 * 512:(w4 + 1) * 512],
                x2b[bi][:, w4 * 512:(w4 + 1) * 512],
                xres[bi][:, w4 * 512:(w4 + 1) * 512], ALU.add), waits=[m_t1])
        m_s12 = P.op("V", lambda e, bi=bi, qt=qt: e.reduce_sum(
            s1_2[:, qt:qt + 1], x2b[bi][:], AX.X),
            waits=[m_y_ev[(qt, w)] for w in range(4)])
        m_sq2 = P.op("S", lambda e, bi=bi, qt=qt: e.activation(
            sqd2[:, 0:2048], x2b[bi][:], AF.Square, accum_out=ssq_2[:, qt:qt + 1]),
            waits=[m_y_ev[(qt, w)] for w in range(4)])
        m_x2_dma[qt] = P.dma("Y", lambda e, qt=qt, bi=bi: e.dma_start(
            out=x2_d[qt * 128:(qt + 1) * 128, :], in_=x2b[bi][:]),
            waits=[m_s12, m_sq2])

    m_m2 = P.op("V", lambda e: e.tensor_scalar(
        mean2[:], s1_2[:], 1.0 / 2048.0, None, ALU.mult), waits=[m_x2_dma[NT - 1]])
    m_v2a = P.op("V", lambda e: e.scalar_tensor_tensor(
        var2[:], mean2[:], -1.0, mean2[:], ALU.mult, ALU.mult), waits=[m_m2])
    m_v2b = P.op("V", lambda e: e.scalar_tensor_tensor(
        var2[:], ssq_2[:], 1.0 / 2048.0, var2[:], ALU.mult, ALU.add), waits=[m_v2a])
    m_sd2 = P.op("S", lambda e: e.activation(var2[:], var2[:], AF.Sqrt, bias=eps_sb[:]),
                 waits=[m_v2b])
    m_rs2 = P.op("V", lambda e: e.reciprocal(rstd2[:], var2[:]), waits=[m_sd2])
    m_nm2 = P.op("V", lambda e: e.scalar_tensor_tensor(
        negmr2[:], mean2[:], -1.0, rstd2[:], ALU.mult, ALU.mult), waits=[m_rs2])

    staging = arenaB
    stg_gu = staging.rearrange("p (j f) -> p j f", j=CJ)
    stg_d = staging.rearrange("p (j f) -> p j f", j=IJ)
    cbuf = [arenaC[:, 0:8192], arenaC[:, 8192:16384]]
    m_ws_done = [None] * 3
    wpass_free = m_y_ev[(NT - 1, 3)]
    staging_free = m_wo_last_mm
    prev_st_dma = None
    m_st_last = None

    for (wsrc, wdst, mi, kind) in [(w_gate, wqg_d, 0, "gu"), (w_up, wqu_d, 1, "gu"),
                                   (w_down, wqd_d, 2, "d")]:
        ntile = 64 if kind == "gu" else 16
        m_red = [None] * ntile
        for t in range(ntile):
            bi = t % 2
            ww = [m_red[t - 2]] if t >= 2 else [wpass_free]
            if kind == "d" and t < 2:
                ww.append(m_x2_dma[NT - 1])
                ww.append(m_x2_dma[NT - 2])
            if kind == "gu":
                mwp = P.dma("Y", lambda e, t=t, bi=bi, wsrc=wsrc: e.dma_start(
                    out=wpass[bi][:], in_=wsrc[t * 128:(t + 1) * 128, :]), waits=ww)
                m_red[t] = P.op("V", lambda e, t=t, bi=bi: e.reduce_sum(
                    wsum_p[:, t:t + 1], wpass[bi][:], AX.X, apply_absolute_value=True),
                    waits=[mwp])
            else:
                mwp = P.dma("Y", lambda e, t=t, bi=bi, wsrc=wsrc: e.dma_start(
                    out=cbuf[bi], in_=wsrc[t * 128:(t + 1) * 128, :]), waits=ww)
                m_red[t] = P.op("V", lambda e, t=t, bi=bi: e.reduce_sum(
                    wsum_p[:, t:t + 1], cbuf[bi], AX.X, apply_absolute_value=True),
                    waits=[mwp])
        m_rall = P.op("V", lambda e, ntile=ntile: e.reduce_sum(
            wsum_1[:], wsum_p[:, 0:ntile], AX.X), waits=[m_red[ntile - 1]])
        mm_s = P.op("T", lambda e: e.matmul(
            ps_aux[0:1, 0:1], ones_sb[:], wsum_1[:], start=True, stop=True),
            waits=[m_rall])
        m_ws = P.op("S", lambda e: e.activation(
            ws_tmp[:], ps_aux[0:1, 0:1], AF.Identity, bias=eps_sb[0:1, :],
            scale=1.0 / float(I * C)), waits=[mm_s])
        m_rw = P.op("V", lambda e: e.reciprocal(rws_tmp[:], ws_tmp[:]), waits=[m_ws])
        mm_b1 = P.op("T", lambda e: e.matmul(
            ps_aux[:, 0:1], onesr_sb[:], rws_tmp[:], start=True, stop=True), waits=[m_rw])
        m_rwbc = P.op("V", lambda e, mi=mi: e.tensor_copy(rws_bc[mi][:], ps_aux[:, 0:1]),
                      waits=[mm_b1])
        mm_b2 = P.op("T", lambda e: e.matmul(
            ps_aux[:, 0:1], onesr_sb[:], ws_tmp[:], start=True, stop=True), waits=[m_rwbc])
        m_wsbc = P.op("V", lambda e, mi=mi: e.tensor_copy(ws_bc[mi][:], ps_aux[:, 0:1]),
                      waits=[mm_b2])
        m_ws_done[mi] = m_wsbc

        m_q_ev = {}
        m_prev_sc = None
        m_prev_tr = {}
        if kind == "gu":
            nwin, tper, stg, wid = 16, 4, stg_gu, C
        else:
            nwin, tper, stg, wid = 8, 2, stg_d, I
        npc = wid // 1024
        for win in range(nwin):
            for tw in range(tper):
                t = win * tper + tw
                bi = t % 2
                ww = [m_q_ev[t - 2]] if t >= 2 else [m_red[ntile - 1]]
                buf = (wpass[bi] if kind == "gu" else cbuf[bi])
                m_ld = P.dma("Y", lambda e, t=t, buf=buf, wsrc=wsrc: e.dma_start(
                    out=buf, in_=wsrc[t * 128:(t + 1) * 128, :]), waits=ww)
                m_last_ev = None
                for pc in range(npc):
                    wh = pc % 2
                    m_sc = P.op("S", lambda e, buf=buf, mi=mi, pc=pc: e.activation(
                        i8_tmp[:], buf[:, pc * 1024:(pc + 1) * 1024], AF.Identity,
                        scale=rws_bc[mi][:]),
                        waits=[m_ld, m_ws_done[mi], m_prev_sc])
                    m_cl = P.op("V", lambda e, wh=wh: e.tensor_scalar(
                        wtile_bf[:, wh * 1024:(wh + 1) * 1024], i8_tmp[:],
                        1.0, -1.0, ALU.min, ALU.max),
                        waits=[m_sc, m_prev_tr.get(wh)])
                    m_prev_sc = m_cl
                    mtr = None
                    for j8 in range(8):
                        trw = []
                        if j8 == 0:
                            trw = [m_cl]
                            trw.append(m_last_ev if m_last_ev is not None else
                                       (m_q_ev.get(t - 1) if t >= 1 else m_st_last))
                            if t == 0 and pc == 0:
                                trw.append(staging_free)
                        mtr = P.op("T", lambda e, wh=wh, j8=j8: e.transpose(
                            ps_trA[:, j8 * 128:(j8 + 1) * 128],
                            wtile_bf[:, wh * 1024 + j8 * 128: wh * 1024 + (j8 + 1) * 128],
                            ident_bf[:]), waits=trw)
                    m_prev_tr[wh] = mtr
                    evw = [mtr]
                    if tw == 0 and pc == 0 and prev_st_dma is not None:
                        evw.append(prev_st_dma)
                    m_last_ev = P.op("V", lambda e, pc=pc, tw=tw, stg=stg: e.tensor_copy(
                        stg[:, pc * 8:(pc + 1) * 8, tw * 128:(tw + 1) * 128],
                        ps_tr3[:, :, :]), waits=evw)
                m_q_ev[t] = m_last_ev
            if kind == "gu":
                prev_st_dma = P.dma("G", lambda e, win=win, wdst=wdst: e.dma_start(
                    out=wdst[win, :, :, :].rearrange("j p f -> p j f"),
                    in_=stg_gu[:, :, 0:512]), waits=[m_q_ev[win * tper + tper - 1]])
            else:
                prev_st_dma = P.dma("G", lambda e, win=win, wdst=wdst: e.dma_start(
                    out=wdst[win, :, :, :].rearrange("j p f -> p j f"),
                    in_=stg_d[:, :, 0:256]), waits=[m_q_ev[win * tper + tper - 1]])
        m_st_last = prev_st_dma
        wpass_free = m_q_ev[ntile - 1] if kind == "gu" else m_q_ev[ntile - 1]
    m_wq_all_done = prev_st_dma

    slabs = [arenaA[:, 0:8192], arenaA[:, 8192:16384]]
    slab3 = [s.rearrange("p (j f) -> p j f", j=CJ) for s in slabs]
    hqT = arenaA[:, 16384:32768]
    hqT3 = hqT.rearrange("p (j q) -> p j q", j=IJ)
    hbuf = [arenaC[:, 0:8192], arenaC[:, 8192:16384]]
    hq_bf = arenaB[:, 0:8192]
    dslabs = [arenaB[:, 0:16384], arenaB[:, 16384:32768]]
    dslab3 = [s.rearrange("p (j f) -> p j f", j=IJ) for s in dslabs]
    x2t = [arenaC[:, 0:2048], arenaC[:, 2048:4096]]
    outb = [arenaC[:, 4096:6144], arenaC[:, 6144:8192]]
    xqT3 = xqT.rearrange("p (j q) -> p j q", j=CJ)

    m_down_prev = None
    m_out_prev = None
    m_hb_prev = None
    prev_stripe_gu = None
    m_hq_done = [None, None]
    arenaA_free = m_wo_last_mm

    for s_ in range(NSTRIPE):
        m_xq_all = []
        m_xqb_prev = None
        for qtl in range(2):
            qt = s_ * 2 + qtl
            ww = [m_nm2]
            if s_ == 0:
                ww.append(m_wq_all_done)
            if m_down_prev is not None:
                ww.append(m_down_prev)
            if m_out_prev is not None:
                ww.append(m_out_prev)
            m_x2l = P.dma("Y", lambda e, qt=qt, qtl=qtl: e.dma_start(
                out=hbuf[qtl][0:128, 0:2048], in_=x2_d[qt * 128:(qt + 1) * 128, :]),
                waits=ww)
            m_xn2 = P.op("S", lambda e, qt=qt, qtl=qtl: e.activation(
                hbuf[qtl][:, 2048:4096], hbuf[qtl][:, 0:2048], AF.Identity,
                bias=negmr2[:, qt:qt + 1], scale=rstd2[:, qt:qt + 1]),
                waits=[m_x2l, m_nm2])
            m_mx = P.op("V", lambda e, qtl=qtl: e.reduce_max(
                mx_q[:], hbuf[qtl][:, 2048:4096], AX.X, apply_absolute_value=True),
                waits=[m_xn2])
            m_sx = P.op("V", lambda e, qtl=qtl: e.tensor_scalar(
                sx_q[qtl][:], mx_q[:], 1.0 / 127.0, EPS, ALU.mult, ALU.add), waits=[m_mx])
            m_rsx = P.op("V", lambda e, qtl=qtl: e.reciprocal(rsx_q[qtl][:], sx_q[qtl][:]),
                         waits=[m_sx])
            m_xqb = None
            for pc in range(2):
                m_xq8 = P.op("S", lambda e, qtl=qtl, pc=pc: e.activation(
                    i8_tmp[:], hbuf[qtl][:, 2048 + pc * 1024: 2048 + (pc + 1) * 1024],
                    AF.Identity, scale=rsx_q[qtl][:]),
                    waits=[m_rsx] + ([m_xqb_prev] if m_xqb_prev else []))
                m_xqb = P.op("V", lambda e, pc=pc: e.tensor_copy(
                    xq_bf[:, pc * 1024:(pc + 1) * 1024], i8_tmp[:]), waits=[m_xq8])
                m_xqb_prev = m_xqb
            P.op("V", lambda e, qtl=qtl: e.tensor_tensor(
                sg_q[qtl][:], sx_q[qtl][:], ws_bc[0][:], ALU.mult), waits=[m_sx])
            P.op("V", lambda e, qtl=qtl: e.tensor_tensor(
                su_q[qtl][:], sx_q[qtl][:], ws_bc[1][:], ALU.mult), waits=[m_sx])
            for g in range(2):
                mtr = None
                for j8 in range(8):
                    j = g * 8 + j8
                    trw = [m_xqb]
                    if s_ == 0 and qtl == 0 and g == 0:
                        trw.append(arenaA_free)
                    if m_xq_all:
                        trw.append(m_xq_all[-1])
                    mtr = P.op("T", lambda e, j=j, j8=j8: e.transpose(
                        ps_trA[:, j8 * 128:(j8 + 1) * 128],
                        xq_bf[:, j * 128:(j + 1) * 128], ident_bf[:]),
                        waits=trw if j8 == 0 else ())
                m_ev = P.op("V", lambda e, g=g, qtl=qtl: e.tensor_copy(
                    xqT3[:, g * 8:(g + 1) * 8, qtl * 128:(qtl + 1) * 128],
                    ps_tr3[:, :, :]), waits=[mtr])
                m_xq_all.append(m_ev)
            m_xqb_prev = m_xq_all[-1]

        m_gu_ev = {}
        m_h_last = [None, None]
        m_slab = {}
        for win in range(16):
            for gi, wdsrc in [(0, wqg_d), (1, wqu_d)]:
                idx = win * 2 + gi
                sbuf_i = idx % 2
                ww = []
                if idx >= 2:
                    ww = [m_gu_ev[(idx - 2, 1)]]
                elif s_ > 0:
                    ww = [prev_stripe_gu]
                else:
                    ww = [m_wq_all_done, arenaA_free]
                m_slab[idx] = P.dma("G", lambda e, win=win, wdsrc=wdsrc, sbuf_i=sbuf_i: e.dma_start(
                    out=slab3[sbuf_i][:, :, 0:512],
                    in_=wdsrc[win, :, :, :].rearrange("j p f -> p j f")), waits=ww)
                for qtl in range(2):
                    mm = None
                    w0 = m_xq_all + [m_slab[idx]]
                    if idx >= 1:
                        w0.append(m_gu_ev[(idx - 1, qtl)])
                    elif s_ > 0:
                        w0.append(prev_stripe_gu)
                    for j in range(CJ):
                        mm = P.op("T", lambda e, j=j, qtl=qtl, sbuf_i=sbuf_i, gi=gi: e.matmul(
                            ps_sc[qtl][:, gi * 512:(gi + 1) * 512],
                            xqT[:, j * 256 + qtl * 128: j * 256 + (qtl + 1) * 128],
                            slabs[sbuf_i][:, j * 512:(j + 1) * 512],
                            start=(j == 0), stop=(j == CJ - 1)),
                            waits=w0 if j == 0 else ())
                    if gi == 0:
                        m_gu_ev[(idx, qtl)] = P.op("S", lambda e, qtl=qtl, win=win: e.activation(
                            hbuf[qtl][:, win * 512:(win + 1) * 512], ps_sc[qtl][:, 0:512],
                            AF.Silu, scale=sg_q[qtl][:]), waits=[mm])
                    else:
                        m_up = P.op("S", lambda e, qtl=qtl: e.activation(
                            up_t[qtl][:], ps_sc[qtl][:, 512:1024], AF.Identity,
                            scale=su_q[qtl][:]), waits=[mm])
                        m_h = P.op("V", lambda e, qtl=qtl, win=win: e.tensor_tensor(
                            hbuf[qtl][:, win * 512:(win + 1) * 512],
                            hbuf[qtl][:, win * 512:(win + 1) * 512],
                            up_t[qtl][:], ALU.mult),
                            waits=[m_up, m_gu_ev[(idx - 1, qtl)]])
                        P.op("V", lambda e, qtl=qtl, win=win: e.reduce_max(
                            hmx[:, qtl * 16 + win: qtl * 16 + win + 1],
                            hbuf[qtl][:, win * 512:(win + 1) * 512], AX.X,
                            apply_absolute_value=True), waits=[m_h])
                        m_gu_ev[(idx, qtl)] = m_h
                        m_h_last[qtl] = m_h
        prev_stripe_gu = m_gu_ev[(31, 1)]

        m_h8_prev = m_xqb_prev
        for qtl in range(2):
            m_hm = P.op("V", lambda e, qtl=qtl: e.reduce_max(
                hmx1[qtl][:], hmx[:, qtl * 16:(qtl + 1) * 16], AX.X),
                waits=[m_h_last[qtl]])
            m_sh = P.op("V", lambda e, qtl=qtl: e.tensor_scalar(
                sh_q[qtl][:], hmx1[qtl][:], 1.0 / 127.0, EPS, ALU.mult, ALU.add),
                waits=[m_hm])
            m_rsh = P.op("V", lambda e, qtl=qtl: e.reciprocal(
                rsh_q[qtl][:], sh_q[qtl][:]), waits=[m_sh])
            m_shd = P.op("V", lambda e, qtl=qtl: e.tensor_tensor(
                sh_q[qtl][:], sh_q[qtl][:], ws_bc[2][:], ALU.mult), waits=[m_rsh])
            m_hb = None
            for pc in range(8):
                w8 = [m_rsh, m_h_last[qtl]]
                if m_h8_prev is not None:
                    w8.append(m_h8_prev)
                m_h8 = P.op("S", lambda e, qtl=qtl, pc=pc: e.activation(
                    i8_tmp[:], hbuf[qtl][:, pc * 1024:(pc + 1) * 1024], AF.Identity,
                    scale=rsh_q[qtl][:]), waits=w8)
                hbw = [m_h8]
                if s_ == 0 and qtl == 0 and pc == 0:
                    hbw.append(m_wq_all_done)
                if m_hb_prev is not None and pc == 0:
                    hbw.append(m_hb_prev)
                m_hb = P.op("V", lambda e, pc=pc: e.tensor_copy(
                    hq_bf[:, pc * 1024:(pc + 1) * 1024], i8_tmp[:]), waits=hbw)
                m_h8_prev = m_hb
            m_hq = None
            for g in range(8):
                mtr = None
                for j8 in range(8):
                    j = g * 8 + j8
                    trw = [m_hb]
                    if m_hq is not None:
                        trw.append(m_hq)
                    mtr = P.op("T", lambda e, j=j, j8=j8: e.transpose(
                        ps_trA[:, j8 * 128:(j8 + 1) * 128],
                        hq_bf[:, j * 128:(j + 1) * 128], ident_bf[:]),
                        waits=trw if j8 == 0 else ())
                m_hq = P.op("V", lambda e, g=g, qtl=qtl: e.tensor_copy(
                    hqT3[:, g * 8:(g + 1) * 8, qtl * 128:(qtl + 1) * 128],
                    ps_tr3[:, :, :]), waits=[mtr])
            m_hq_done[qtl] = m_hq

        m_x2r = [None, None]
        for qtl in range(2):
            qt = s_ * 2 + qtl
            m_x2r[qtl] = P.dma("Y", lambda e, qt=qt, qtl=qtl: e.dma_start(
                out=x2t[qtl], in_=x2_d[qt * 128:(qt + 1) * 128, :]),
                waits=[m_hq_done[0], m_hq_done[1]])

        m_dn_ev = {}
        m_dslab = [None] * 8
        for win in range(8):
            sbuf_i = win % 2
            ww = [m_hq_done[0], m_hq_done[1]]
            if win >= 2:
                ww.append(m_dn_ev[(win - 2, 1)])
            m_dslab[win] = P.dma("G", lambda e, win=win, sbuf_i=sbuf_i: e.dma_start(
                out=dslab3[sbuf_i][:, :, 0:256],
                in_=wqd_d[win, :, :, :].rearrange("j p f -> p j f")), waits=ww)
            for qtl in range(2):
                mm = None
                w0 = [m_dslab[win], m_hq_done[qtl]]
                if win >= 1:
                    w0.append(m_dn_ev[(win - 1, qtl)])
                for j in range(IJ):
                    mm = P.op("T", lambda e, j=j, qtl=qtl, sbuf_i=sbuf_i: e.matmul(
                        ps_pv[:, qtl * 512: qtl * 512 + 256],
                        hqT[:, j * 256 + qtl * 128: j * 256 + (qtl + 1) * 128],
                        dslabs[sbuf_i][:, j * 256:(j + 1) * 256],
                        start=(j == 0), stop=(j == IJ - 1)),
                        waits=w0 if j == 0 else ())
                dw = [mm, m_shd]
                if win == 0 and m_out_prev is not None:
                    dw.append(m_out_prev)
                m_d1 = P.op("S", lambda e, qtl=qtl, win=win: e.activation(
                    outb[qtl][:, win * 256:(win + 1) * 256],
                    ps_pv[:, qtl * 512: qtl * 512 + 256], AF.Identity,
                    scale=sh_q[qtl][:]), waits=dw)
                m_dn_ev[(win, qtl)] = P.op("V", lambda e, qtl=qtl, win=win: e.tensor_tensor(
                    outb[qtl][:, win * 256:(win + 1) * 256],
                    outb[qtl][:, win * 256:(win + 1) * 256],
                    x2t[qtl][:, win * 256:(win + 1) * 256], ALU.add),
                    waits=[m_d1, m_x2r[qtl]])
        m_hb_prev = m_dslab[7]
        for qtl in range(2):
            qt = s_ * 2 + qtl
            m_out_prev = P.dma("Y", lambda e, qt=qt, qtl=qtl: e.dma_start(
                out=out_d[qt * 128:(qt + 1) * 128, :], in_=outb[qtl][:]),
                waits=[m_dn_ev[(7, qtl)], m_dn_ev[(6, qtl)]])
        m_down_prev = m_dn_ev[(7, 1)]

    if DEBUG:
        lastw = [m_out_prev, m_down_prev]
        P.dma("Y", lambda e: e.dma_start(out=dbg["dbg_x2"][:, :], in_=x2_d[:, :]),
              waits=lastw)
        P.dma("Y", lambda e: e.dma_start(out=dbg["dbg_wqg"][:, :, :, :],
                                         in_=wqg_d[:, :, :, :]), waits=lastw)
        P.dma("Y", lambda e: e.dma_start(out=dbg["dbg_wqd"][:, :, :, :],
                                         in_=wqd_d[:, :, :, :]), waits=lastw)
        P.dma("G", lambda e: e.dma_start(out=dbg["dbg_xqT"][:, :], in_=xqT[:]),
              waits=lastw)
        def statdump(e):
            return e.dma_start(out=dbg["dbg_stats"][:, :], in_=stat_cat[:])
        P.dma("Y", lambda e: e.dma_start(out=dbg["dbg_stats"][:, 0:16], in_=s1_2[:]),
              waits=lastw)
        P.dma("Y", lambda e: e.dma_start(out=dbg["dbg_stats"][:, 16:32], in_=rstd2[:]),
              waits=lastw)
        P.dma("Y", lambda e: e.dma_start(out=dbg["dbg_stats"][:, 32:48], in_=negmr2[:]),
              waits=lastw)
        for i in range(3):
            P.dma("Y", lambda e, i=i: e.dma_start(out=dbg["dbg_stats"][:, 48 + i:49 + i],
                                                  in_=ws_bc[i][:], allow_slow_non_contiguous=True), waits=lastw)
        for i in range(2):
            P.dma("Y", lambda e, i=i: e.dma_start(out=dbg["dbg_stats"][:, 52 + i:53 + i],
                                                  in_=sh_q[i][:], allow_slow_non_contiguous=True), waits=lastw)
            P.dma("Y", lambda e, i=i: e.dma_start(out=dbg["dbg_stats"][:, 54 + i:55 + i],
                                                  in_=sx_q[i][:], allow_slow_non_contiguous=True), waits=lastw)

    final_dy = P.cnt.get("DY", 0)

    sem_stack = ExitStack()
    sems = {}
    for i_ in range(8):
        sem_stack.enter_context(nc.semaphore(f"dummy{i_}"))
    for name in ("T", "V", "S", "G", "DY", "DG"):
        if name in P.cnt:
            sems[name] = sem_stack.enter_context(nc.semaphore(name + "_sem"))
    if SERIAL2[0] or FRONTIER[0]:
        sems["SEQ"] = sem_stack.enter_context(nc.semaphore("SEQ_sem"))
    block = st.enter_context(nc.Block())

    def emit(eng_name, eng):
        own = eng_name if eng_name != "Y" else None
        if SERIAL2[0]:
            for (waits, fn, sem, inc, gidx, fr) in P.ops[eng_name]:
                if gidx > 0:
                    eng.wait_ge(sems["SEQ"], gidx)
                instr = fn(eng)
                instr.then_inc(sems["SEQ"], inc)
            if eng_name == "Y":
                eng.wait_ge(sems["SEQ"], P.gidx)
            return
        if FRONTIER[0]:
            for (waits, fn, sem, inc, gidx, fr) in P.ops[eng_name]:
                if fr > 0:
                    eng.wait_ge(sems["SEQ"], fr)
                instr = fn(eng)
                instr.then_inc(sems["SEQ"], inc)
            if eng_name == "Y":
                eng.wait_ge(sems["SEQ"], P.gidx)
            return
        for (waits, fn, sem, inc, gidx, fr) in P.ops[eng_name]:
            for (ws, wv) in waits:
                if ws == own:
                    continue
                eng.wait_ge(sems[ws], wv)
            instr = fn(eng)
            instr.then_inc(sems[sem], inc)
        if eng_name == "Y":
            eng.wait_ge(sems["DY"], final_dy)

    @block.sync
    def _(e):
        emit("Y", e)

    @block.tensor
    def _(e):
        emit("T", e)

    @block.vector
    def _(e):
        emit("V", e)

    @block.scalar
    def _(e):
        emit("S", e)

    @block.gpsimd
    def _(e):
        emit("G", e)

    sem_stack.close()
    st.close()
    _PLANBOX[0] = P
    return nc


_NC_CACHE = [None]


def kernel(L0_attn_x, L0_attn_Wq, L0_attn_bq, L0_attn_Wk, L0_attn_bk, L0_attn_Wv,
           L0_attn_bv, L0_attn_Wo, L0_attn_bo, L0_ffn_w_gate, L0_ffn_w_up, L0_ffn_w_down):
    x = np.ascontiguousarray(np.asarray(L0_attn_x, dtype=np.float32))
    Wq = np.asarray(L0_attn_Wq, np.float32)
    Wk = np.asarray(L0_attn_Wk, np.float32)
    Wv = np.asarray(L0_attn_Wv, np.float32)
    Wo = np.asarray(L0_attn_Wo, np.float32)
    bq = np.asarray(L0_attn_bq, np.float32)
    bk = np.asarray(L0_attn_bk, np.float32)
    bv = np.asarray(L0_attn_bv, np.float32)
    bo = np.asarray(L0_attn_bo, np.float32)
    wg = np.ascontiguousarray(np.asarray(L0_ffn_w_gate, np.float32))
    wu = np.ascontiguousarray(np.asarray(L0_ffn_w_up, np.float32))
    wd = np.ascontiguousarray(np.asarray(L0_ffn_w_down, np.float32))

    wq_r = np.ascontiguousarray(
        Wq.reshape(CJ, 128, H, DH).transpose(2, 1, 0, 3).reshape(H, 128, C))
    wk_r = np.ascontiguousarray(Wk.reshape(CJ, 128, DH).transpose(1, 0, 2).reshape(128, C))
    wv_r = np.ascontiguousarray(Wv.reshape(CJ, 128, DH).transpose(1, 0, 2).reshape(128, C))
    wo_r = np.ascontiguousarray(Wo.reshape(CJ, 128, C))
    bq_c = np.ascontiguousarray(bq.reshape(H, 128).T)

    if _NC_CACHE[0] is None:
        _NC_CACHE[0] = _build()
    nc = _NC_CACHE[0]
    in_maps = []
    for core in range(8):
        b, half = core // 2, core % 2
        x_roll = np.ascontiguousarray(np.roll(x[b], -half * 2048, axis=0))
        in_maps.append({
            "x_full": x_roll,
            "wq_r": wq_r, "wk_r": wk_r, "wv_r": wv_r, "wo_r": wo_r,
            "w_gate": wg, "w_up": wu, "w_down": wd,
            "bq_c": bq_c, "bk_c": bk.reshape(128, 1), "bv_c": bv.reshape(128, 1),
            "bo_r": bo.reshape(1, C),
            "ones_c": np.ones((128, 1), np.float32),
            "ones_r": np.ones((1, 128), np.float32),
            "ident_h": np.eye(128, dtype=np.float32),
        })
    res = run_bass_kernel_spmd(nc, in_maps, core_ids=list(range(8)))
    out = np.empty((B, S, C), np.float32)
    for core in range(8):
        b, half = core // 2, core % 2
        out[b, half * 2048:(half + 1) * 2048, :] = res.results[core]["out"]
    return out
